# version 01
# baseline (speedup 1.0000x reference)
import numpy as np
import ml_dtypes
from contextlib import ExitStack

import concourse.bass as bass
import concourse.mybir as mybir
from concourse.bass_utils import run_bass_kernel_spmd

F32 = mybir.dt.float32
BF16 = mybir.dt.bfloat16
I32 = mybir.dt.int32
AF = mybir.ActivationFunctionType
ALU = mybir.AluOpType
AX = mybir.AxisListType.X

NCORE = 8
B = 2
T = 4096
D = 2048
HID = 5120
NCH = D // 128
TOK = 1024
TTILE = 512
HD = 64
WCNT = float(HID * D)
LN_EPS = 1e-5

IDX_KDUP = 0
IDX_V65 = 16
IDX_Q = 48
IDX_AO = 80


def build_nc(stage=4, debug=False):
    nc = bass.Bass("TRN2", target_bir_lowering=False, debug=False)

    ap = lambda t: t.ap()
    x_t = ap(nc.dram_tensor("x_t", [D, TOK], F32, kind="ExternalInput"))
    wq = ap(nc.dram_tensor("wq", [D, D], F32, kind="ExternalInput"))
    wk = ap(nc.dram_tensor("wk", [D, 512], F32, kind="ExternalInput"))
    wv = ap(nc.dram_tensor("wv", [D, 512], F32, kind="ExternalInput"))
    wo = ap(nc.dram_tensor("wo", [D, D], F32, kind="ExternalInput"))
    wfc_sh = ap(nc.dram_tensor("wfc_sh", [256, HID], F32, kind="ExternalInput"))
    wpr_sh = ap(nc.dram_tensor("wpr_sh", [640, D], F32, kind="ExternalInput"))
    bq_r = ap(nc.dram_tensor("bq_r", [128, NCH], F32, kind="ExternalInput"))
    bk_r = ap(nc.dram_tensor("bk_r", [128, 4], F32, kind="ExternalInput"))
    bv_r = ap(nc.dram_tensor("bv_r", [128, 4], F32, kind="ExternalInput"))
    bo_r = ap(nc.dram_tensor("bo_r", [128, NCH], F32, kind="ExternalInput"))
    costab = ap(nc.dram_tensor("costab", [128, TTILE], F32, kind="ExternalInput"))
    sintab = ap(nc.dram_tensor("sintab", [128, TTILE], F32, kind="ExternalInput"))
    maskin = ap(nc.dram_tensor("maskin", [128, 4 * TTILE], BF16, kind="ExternalInput"))
    rotin = ap(nc.dram_tensor("rotin", [128, 128], BF16, kind="ExternalInput"))
    idin = ap(nc.dram_tensor("idin", [128, 128], BF16, kind="ExternalInput"))
    idxin = ap(nc.dram_tensor("idxin", [128, 112], I32, kind="ExternalInput"))

    xo = ap(nc.dram_tensor("xo", [D, TOK], F32, kind="ExternalOutput"))
    if debug:
        dbg_srow = ap(nc.dram_tensor("dbg_srow", [1, 16], F32, kind="ExternalOutput"))
        dbg_thr = ap(nc.dram_tensor("dbg_thr", [128, 4], F32, kind="ExternalOutput"))
        dbg_ln = ap(nc.dram_tensor("dbg_ln", [128, TOK], BF16, kind="ExternalOutput"))
        dbg_q1g = ap(nc.dram_tensor("dbg_q1g", [D, HID], BF16, kind="ExternalOutput"))
        dbg_kn = ap(nc.dram_tensor("dbg_kn", [1, 1], F32, kind="ExternalOutput"))
        dbg_cb = ap(nc.dram_tensor("dbg_cb", [128, 2], F32, kind="ExternalOutput"))
        dbg_qin = ap(nc.dram_tensor("dbg_qin", [D, TOK], BF16, kind="ExternalOutput"))
        dbg_kin = ap(nc.dram_tensor("dbg_kin", [520, TOK], BF16, kind="ExternalOutput"))
        dbg_ain = ap(nc.dram_tensor("dbg_ain", [512, T], BF16, kind="ExternalOutput"))
        dbg_kqs = ap(nc.dram_tensor("dbg_kqs", [128, TOK], BF16, kind="ExternalOutput"))
        dbg_ps = ap(nc.dram_tensor("dbg_ps", [128, 512], F32, kind="ExternalOutput"))
        dbg_wt = ap(nc.dram_tensor("dbg_wt", [128, 2048], BF16, kind="ExternalOutput"))
        dbg_akd = ap(nc.dram_tensor("dbg_akd", [128, 4096], BF16, kind="ExternalOutput"))
        dbg_aq = ap(nc.dram_tensor("dbg_aq", [128, 4096], BF16, kind="ExternalOutput"))
        dbg_v65 = ap(nc.dram_tensor("dbg_v65", [128, 520], BF16, kind="ExternalOutput"))

    q1c = ap(nc.dram_tensor("q1c", [256, HID], BF16))
    q1g = ap(nc.dram_tensor("q1g", [D, HID], BF16))
    q2c = ap(nc.dram_tensor("q2c", [640, D], BF16))
    q2g = ap(nc.dram_tensor("q2g", [HID, D], BF16))
    mri = ap(nc.dram_tensor("mri", [128, 128], F32))
    mro = ap(nc.dram_tensor("mro", [128, 128], F32))
    kin = ap(nc.dram_tensor("kin", [520, TOK], BF16))
    kout = ap(nc.dram_tensor("kout", [8 * 520, TOK], BF16))
    vin = ap(nc.dram_tensor("vin", [TOK, 512], BF16))
    vout = ap(nc.dram_tensor("vout", [8 * TOK, 512], BF16))
    qin = ap(nc.dram_tensor("qin", [D, TOK], BF16))
    qout = ap(nc.dram_tensor("qout", [8 * D, TOK], BF16))
    ain = ap(nc.dram_tensor("ain", [512, T], BF16))
    aout = ap(nc.dram_tensor("aout", [4096, T], BF16))
    knb = ap(nc.dram_tensor("knb", [8, 1], F32))

    steps = []
    cnt = {s: 0 for s in ("DS", "DP", "DV", "CC", "PE", "ACT", "VE")}

    def st(eng, fn=None, w=(), i=()):
        i = list(i)
        if eng == "vector" and fn is not None and not any(s == "VE" for (s, a) in i):
            i.append(("VE", 1))
        incs = []
        for (s, a) in i:
            cnt[s] += a
            incs.append((s, a))
        steps.append((eng, fn, tuple(w), tuple(incs)))

    def val(s):
        return cnt[s]

    es = ExitStack()
    with es:
        sb = lambda name, shp, dt: es.enter_context(nc.sbuf_tensor(name, shp, dt))
        ps = lambda name, shp, dt: es.enter_context(nc.psum_tensor(name, shp, dt))

        x = [sb(f"x{c}", [128, TOK], F32) for c in range(NCH)]
        lnb = [sb(f"ln{c}", [128, TOK], BF16) for c in range(NCH)]
        cost = sb("cost", [128, TTILE], F32)
        sint = sb("sint", [128, TTILE], F32)
        masks = sb("masks", [128, 4 * TTILE], BF16)
        rotp = sb("rotp", [128, 128], BF16)
        ident = sb("ident", [128, 128], BF16)
        sbq = sb("sbq", [128, NCH], F32)
        sbk = sb("sbk", [128, 4], F32)
        sbv = sb("sbv", [128, 4], F32)
        sbo = sb("sbo", [128, NCH], F32)
        sidx = sb("sidx", [128, 112], I32)
        onesc_b = sb("onesc_b", [128, 1], BF16)
        onesc_f = sb("onesc_f", [128, 1], F32)
        onesr_f = sb("onesr_f", [1, 128], F32)
        thr_bc = sb("thr_bc", [128, 4], F32)
        sffn = sb("sffn", [128, 1], F32)
        srow = sb("srow", [1, 16], F32)
        stage128 = sb("stage128", [128, 128], F32)
        knrow = sb("knrow", [1, TOK], BF16)
        knsc = sb("knsc", [8, TOK], BF16)
        knmx = sb("knmx", [8, TOK], F32)
        knmax8 = sb("knmax8", [8, 1], F32)
        knrow8 = sb("knrow8", [1, 8], F32)
        kn2max = sb("kn2max", [1, 1], F32)
        cb = sb("cb", [128, 2], F32)
        tstrip = sb("tstrip", [128, 16], F32)
        rrow_at = sb("rrow_at", [1, TTILE], F32)
        epsv = sb("epsv", [1, 1], F32)
        dbgs = sb("dbgs", [128, 512], F32) if debug else None
        dbgw = sb("dbgw", [128, 2048], BF16) if debug else None

        AR_ELE = 28672
        arena = sb("arena", [128, AR_ELE], BF16)

        def carve_b(off, p, n):
            assert off + n <= AR_ELE
            return arena[0:p, off:off + n]

        def carve_f(off, p, n):
            assert off % 2 == 0 and off + 2 * n <= AR_ELE
            return arena[0:p, off:off + 2 * n].bitcast(F32)

        twt = [carve_f(0, 128, 2048), carve_f(4096, 128, 2048)]
        tqt = [carve_b(8192, 128, 2048), carve_b(10240, 128, 2048)]
        tcl = [carve_f(12288, 128, 2048), carve_f(16384, 128, 2048)]

        lntmp = [carve_f(16384, 128, 512), carve_f(17408, 128, 512)]
        lnxb = [carve_b(18432, 128, 1024), carve_b(19456, 128, 1024)]
        lnxs = [carve_b(20480, 128, 1024), carve_b(21504, 128, 1024)]
        mrow = carve_f(22528, 1, 1024)
        vrow = carve_f(24576, 1, 1024)
        rrow = carve_f(26624, 1, 1024)

        fq1 = [carve_b(0, 128, 2048), carve_b(2048, 128, 2048)]
        fq2 = [carve_b(4096, 128, 1024), carve_b(5120, 128, 1024)]
        fhsq = [carve_b(6144 + 1024 * j, 128, 1024) for j in range(8)]
        frel = [carve_f(14336, 128, 512), carve_f(15360, 128, 512)]

        kwt = [carve_b(0, 128, 2048), carve_b(2048, 128, 2048)]
        kstg = carve_f(18432, 128, 2048)
        kqs = [carve_b(4096, 128, 1024), carve_b(5120, 128, 1024)]
        kro = [carve_b(6144, 128, 1024), carve_b(7168, 128, 1024)]
        kt1 = carve_f(8192, 128, 512)
        kt2 = carve_f(9216, 128, 512)
        ksq = carve_b(10240, 128, 1024)
        kvf = [carve_b(11264 + 1024 * c, 128, 1024) for c in range(4)]
        kvt = [carve_b(15360, 128, 512), carve_b(15872, 128, 512)]

        akd = [carve_b(0, 128, 4096), carve_b(4096, 128, 4096)]
        av65 = [carve_b(8192 + 130 * tc, 128, 130) for tc in range(32)]
        aq = [carve_b(12352, 128, 4096), carve_b(16448, 128, 4096)]
        apt = [carve_b(20544 + 512 * k, 128, 512) for k in range(4)]
        aqsq = carve_b(22592, 128, 512)
        av2 = [carve_b(26176, 128, 128), carve_b(26304, 128, 128)]
        arb = [carve_f(23104, 64, 512), carve_f(24128, 64, 512)]
        aao = [carve_b(25152, 64, 512), carve_b(25664, 64, 512)]

        wao = [carve_b(1024 * c, 128, 1024) for c in range(NCH)]
        wwt = [carve_b(16384, 128, 2048), carve_b(18432, 128, 2048)]
        wstg = carve_f(20480, 128, 2048)

        pb = [ps(f"pb{k}", [128, 512], F32) for k in range(7)]
        pbf = ps("pbf", [128, 1024], BF16)

        sems = {}

        def build_schedule():
            for c in range(NCH):
                st("sync", lambda e, c=c: e.dma_start(out=x[c][:], in_=x_t[128 * c:128 * (c + 1), :]),
                   i=[("DS", 16)])
            for dst, src in ((cost, costab), (sint, sintab), (masks, maskin), (rotp, rotin),
                             (ident, idin), (sbq, bq_r), (sbk, bk_r), (sbv, bv_r), (sbo, bo_r),
                             (sidx, idxin)):
                st("sync", lambda e, dst=dst, src=src: e.dma_start(out=dst[:], in_=src), i=[("DS", 16)])
            ds_x = val("DS")
            st("vector", lambda e: e.memset(onesc_b[:], 1.0))
            st("vector", lambda e: e.memset(onesc_f[:], 1.0))
            st("vector", lambda e: e.memset(onesr_f[:], 1.0))
            st("vector", lambda e: e.memset(stage128[:], 0.0))
            st("vector", lambda e: e.memset(epsv[:], LN_EPS), i=[("VE", 1)])
            ve_ones = val("VE")

            tiles = []
            for pc in range(2):
                for ct in range(4):
                    tiles.append(("fc", wfc_sh[128 * pc:128 * (pc + 1), 1280 * ct:1280 * (ct + 1)],
                                  q1c[128 * pc:128 * (pc + 1), 1280 * ct:1280 * (ct + 1)], 1280))
            for pc in range(5):
                tiles.append(("pr", wpr_sh[128 * pc:128 * (pc + 1), :],
                              q2c[128 * pc:128 * (pc + 1), :], 2048))
            ve_red = {}
            for ti, (kind, src, dst, w) in enumerate(tiles):
                bi = ti % 2
                wts = [("VE", ve_red[ti - 2])] if ti >= 2 else []
                st("gpsimd", lambda e, bi=bi, src=src, w=w: e.dma_start(out=twt[bi][:, 0:w], in_=src),
                   w=wts, i=[("DP", 16)])
                st("vector", lambda e, bi=bi, ti=ti, w=w: e.reduce_sum(
                    tstrip[:, ti:ti + 1], twt[bi][:, 0:w], AX, apply_absolute_value=True),
                   w=[("DP", val("DP"))], i=[("VE", 1)])
                ve_red[ti] = val("VE")
            st("vector", lambda e: e.reduce_sum(tstrip[:, 13:14], tstrip[:, 0:8], AX),
               w=[("VE", val("VE"))], i=[("VE", 1)])
            st("vector", lambda e: e.reduce_sum(tstrip[:, 14:15], tstrip[:, 8:13], AX), i=[("VE", 1)])
            ve_cols = val("VE")
            st("tensor", lambda e: e.matmul(pb[6][0:1, 0:1], tstrip[:, 13:14], onesc_f[:],
                                            start=True, stop=True), w=[("VE", ve_cols)])
            st("tensor", lambda e: e.matmul(pb[6][0:1, 1:2], tstrip[:, 14:15], onesc_f[:],
                                            start=True, stop=True), i=[("PE", 1)])
            pe_sums = val("PE")
            st("vector", lambda e: e.tensor_copy(stage128[0:1, 0:2], pb[6][0:1, 0:2]),
               w=[("PE", pe_sums), ("VE", ve_ones)], i=[("VE", 1)])
            st("scalar", lambda e: e.dma_start(out=mri, in_=stage128[:]), w=[("VE", val("VE"))],
               i=[("DV", 16)])
            st("gpsimd", lambda e: e.collective_compute(
                "AllReduce", ALU.add, replica_groups=[list(range(NCORE))],
                ins=[mri.opt()], outs=[mro.opt()]), w=[("DV", val("DV"))], i=[("CC", 1)])
            st("gpsimd", lambda e: e.dma_start(out=srow[0:1, 0:2], in_=mro[0:1, 0:2]),
               w=[("CC", val("CC"))], i=[("DP", 16)])
            dp_am = val("DP")
            st("vector", lambda e: e.tensor_scalar(srow[0:1, 2:4], srow[0:1, 0:2],
                                                   1.0 / WCNT, 1e-5, ALU.mult, ALU.max),
               w=[("DP", dp_am)], i=[("VE", 1)])
            st("vector", lambda e: e.tensor_scalar_mul(srow[0:1, 4:6], srow[0:1, 2:4], 0.7),
               w=[("VE", val("VE"))], i=[("VE", 1)])
            st("vector", lambda e: e.tensor_mul(srow[0:1, 6:7], srow[0:1, 2:3], srow[0:1, 2:3]),
               w=[("VE", val("VE"))], i=[("VE", 1)])
            st("vector", lambda e: e.tensor_mul(srow[0:1, 7:8], srow[0:1, 6:7], srow[0:1, 3:4]),
               w=[("VE", val("VE"))], i=[("VE", 1)])
            st("vector", lambda e: e.tensor_scalar_mul(srow[0:1, 8:9], srow[0:1, 7:8], 0.5),
               w=[("VE", val("VE"))], i=[("VE", 1)])
            ve_am = val("VE")
            for (col, neg, dstc) in ((4, True, 0), (4, False, 1), (5, True, 2), (5, False, 3)):
                st("tensor", lambda e, col=col: e.matmul(pb[6][:, 0:1], onesr_f[:],
                                                         srow[0:1, col:col + 1],
                                                         start=True, stop=True),
                   w=[("VE", ve_am if dstc == 0 else val("VE"))], i=[("PE", 1)])
                sc = -1.0 if neg else 1.0
                st("vector", lambda e, dstc=dstc, sc=sc: e.tensor_scalar_mul(
                    thr_bc[:, dstc:dstc + 1], pb[6][:, 0:1], sc),
                   w=[("PE", val("PE"))], i=[("VE", 1)])
            st("tensor", lambda e: e.matmul(pb[6][:, 0:1], onesr_f[:], srow[0:1, 8:9],
                                            start=True, stop=True),
               w=[("VE", val("VE"))], i=[("PE", 1)])
            st("vector", lambda e: e.tensor_copy(sffn[:], pb[6][:, 0:1]),
               w=[("PE", val("PE"))], i=[("VE", 1)])
            ve_thr = val("VE")

            for ti, (kind, src, dst, w) in enumerate(tiles):
                bi = ti % 2
                st("gpsimd", lambda e, bi=bi, src=src, w=w: e.dma_start(out=twt[bi][:, 0:w], in_=src),
                   w=[("VE", ve_thr if ti < 2 else val("VE")), ("ACT", val("ACT"))], i=[("DP", 16)])
                n0 = 0 if kind == "fc" else 2
                st("vector", lambda e, bi=bi, w=w, n0=n0: e.tensor_scalar(
                    tcl[bi][:, 0:w], twt[bi][:, 0:w], thr_bc[:, n0:n0 + 1],
                    thr_bc[:, n0 + 1:n0 + 2], ALU.max, ALU.min),
                   w=[("DP", val("DP")), ("VE", ve_thr)])
                st("vector", lambda e, bi=bi, w=w: e.tensor_sub(
                    tcl[bi][:, 0:w], twt[bi][:, 0:w], tcl[bi][:, 0:w]), i=[("VE", 1)])
                st("scalar", lambda e, bi=bi, w=w: e.sign(tqt[bi][:, 0:w], tcl[bi][:, 0:w]),
                   w=[("VE", val("VE")), ("DV", val("DV"))], i=[("ACT", 1)])
                st("scalar", lambda e, bi=bi, dst=dst, w=w: e.dma_start(out=dst, in_=tqt[bi][:, 0:w]),
                   w=[("ACT", val("ACT"))], i=[("DV", 16)])
            dv_tern = val("DV")
            st("gpsimd", lambda e: e.collective_compute(
                "AllGather", ALU.bypass, replica_groups=[list(range(NCORE))],
                ins=[q1c.opt()], outs=[q1g.opt()]), w=[("DV", dv_tern)], i=[("CC", 1)])
            st("gpsimd", lambda e: e.collective_compute(
                "AllGather", ALU.bypass, replica_groups=[list(range(NCORE))],
                ins=[q2c.opt()], outs=[q2g.opt()]), i=[("CC", 1)])
            cc_q = val("CC")

            def emit_ln():
                ve_enter = val("VE")
                pe_chunk = {}
                for c in range(NCH):
                    bi = c % 2
                    wts = [("PE", pe_chunk[c - 2])] if c >= 2 else [("DS", ds_x), ("VE", ve_enter)]
                    st("scalar", lambda e, c=c, bi=bi: e.copy(lnxb[bi][:], x[c][:]), w=wts)
                    st("scalar", lambda e, c=c, bi=bi: e.square(lnxs[bi][:], x[c][:]),
                       i=[("ACT", 1)])
                    act_c = val("ACT")
                    for tt in range(2):
                        mm_w = [("ACT", act_c)]
                        if c == 0:
                            mm_w.append(("VE", ve_enter))
                        st("tensor", lambda e, bi=bi, tt=tt, c=c: e.matmul(
                            pb[tt][0:1, :], onesc_b[:], lnxb[bi][:, TTILE * tt:TTILE * (tt + 1)],
                            start=(c == 0), stop=(c == NCH - 1)), w=mm_w)
                        st("tensor", lambda e, bi=bi, tt=tt, c=c: e.matmul(
                            pb[2 + tt][0:1, :], onesc_b[:], lnxs[bi][:, TTILE * tt:TTILE * (tt + 1)],
                            start=(c == 0), stop=(c == NCH - 1)),
                           i=[("PE", 1)] if tt == 1 else ())
                    pe_chunk[c] = val("PE")
                pe_stats = val("PE")
                for tt in range(2):
                    o = TTILE * tt
                    st("vector", lambda e, tt=tt, o=o: e.tensor_scalar_mul(
                        mrow[0:1, o:o + TTILE], pb[tt][0:1, :], 1.0 / D),
                       w=[("PE", pe_stats)], i=[("VE", 1)])
                    st("vector", lambda e, o=o: e.tensor_mul(
                        vrow[0:1, o:o + TTILE], mrow[0:1, o:o + TTILE], mrow[0:1, o:o + TTILE]),
                       w=[("VE", val("VE"))], i=[("VE", 1)])
                    st("vector", lambda e, tt=tt, o=o: e.scalar_tensor_tensor(
                        vrow[0:1, o:o + TTILE], pb[2 + tt][0:1, :], 1.0 / D,
                        vrow[0:1, o:o + TTILE], ALU.mult, ALU.subtract),
                       w=[("VE", val("VE"))], i=[("VE", 1)])
                    st("scalar", lambda e, o=o: e.activation(
                        rrow[0:1, o:o + TTILE], vrow[0:1, o:o + TTILE], AF.Sqrt, bias=epsv[:]),
                       w=[("VE", val("VE"))], i=[("ACT", 1)])
                    st("vector", lambda e, o=o: e.reciprocal(
                        rrow[0:1, o:o + TTILE], rrow[0:1, o:o + TTILE]),
                       w=[("ACT", val("ACT"))], i=[("VE", 1)])
                ve_rows = val("VE")
                for tt in range(2):
                    o = TTILE * tt
                    st("tensor", lambda e, o=o: e.matmul(
                        pb[4][:], onesr_f[:], mrow[0:1, o:o + TTILE], start=True, stop=True),
                       w=[("VE", ve_rows if tt == 0 else val("VE"))])
                    st("tensor", lambda e, o=o: e.matmul(
                        pb[5][:], onesr_f[:], rrow[0:1, o:o + TTILE], start=True, stop=True),
                       i=[("PE", 1)])
                    pe_bc = val("PE")
                    for c in range(NCH):
                        bi = c % 2
                        st("vector", lambda e, c=c, bi=bi, o=o: e.tensor_sub(
                            lntmp[bi][:], x[c][:, o:o + TTILE], pb[4][:]),
                           w=[("PE", pe_bc)] if c == 0 else ())
                        st("vector", lambda e, c=c, bi=bi, o=o: e.tensor_mul(
                            lnb[c][:, o:o + TTILE], lntmp[bi][:], pb[5][:]),
                           i=[("VE", 1)] if c == NCH - 1 else ())
                return val("VE")

            def emit_ffn(ve_ln_done):
                st("tensor", fn=None, w=[("CC", cc_q), ("VE", ve_ln_done)])
                for sup in range(5):
                    last_sq = {0: 0, 1: 0}
                    for jh in range(8):
                        ht = sup * 8 + jh
                        bi = ht % 2
                        ld_w = [("PE", val("PE"))] if ht >= 2 else [("CC", cc_q), ("VE", ve_ln_done)]
                        st("sync", lambda e, ht=ht, bi=bi: e.dma_start(
                            out=fq1[bi][:].rearrange("p (ci f) -> p ci f", ci=16),
                            in_=q1g.rearrange("(ci p) h -> p ci h", p=128)[:, :, 128 * ht:128 * (ht + 1)]),
                           w=ld_w, i=[("DS", 16)])
                        ds_q1 = val("DS")
                        for tt in range(2):
                            pp = pb[tt]
                            for ci in range(NCH):
                                wts = [("DS", ds_q1), ("ACT", val("ACT"))] if ci == 0 else []
                                st("tensor", lambda e, pp=pp, bi=bi, ci=ci, tt=tt: e.matmul(
                                    pp[:], fq1[bi][:, 128 * ci:128 * (ci + 1)],
                                    lnb[ci][:, TTILE * tt:TTILE * (tt + 1)],
                                    start=(ci == 0), stop=(ci == NCH - 1)),
                                   w=wts, i=[("PE", 1)] if ci == NCH - 1 else ())
                            pe_mm = val("PE")
                            st("scalar", lambda e, pp=pp, tt=tt: e.activation(
                                frel[tt][:], pp[:], AF.Relu),
                               w=[("PE", pe_mm), ("VE", last_sq[tt])], i=[("ACT", 1)])
                            st("vector", lambda e, jh=jh, tt=tt: e.tensor_mul(
                                fhsq[jh][:, TTILE * tt:TTILE * (tt + 1)], frel[tt][:], frel[tt][:]),
                               w=[("ACT", val("ACT"))], i=[("VE", 1)])
                            last_sq[tt] = val("VE")
                    ve_hsq = val("VE")
                    for dt in range(NCH):
                        bi = dt % 2
                        st("sync", lambda e, sup=sup, dt=dt, bi=bi: e.dma_start(
                            out=fq2[bi][:].rearrange("p (hb f) -> p hb f", hb=8),
                            in_=q2g.rearrange("(hb p) d -> p hb d", p=128)[:, 8 * sup:8 * (sup + 1),
                                                                          128 * dt:128 * (dt + 1)]),
                           w=[("PE", val("PE"))] if dt >= 2 else [("VE", ve_hsq)], i=[("DS", 16)])
                        ds_q2 = val("DS")
                        for tt in range(2):
                            pp = pb[2 + (2 * dt + tt) % 2]
                            for jh in range(8):
                                wts = [("DS", ds_q2), ("VE", ve_hsq)] if jh == 0 else []
                                st("tensor", lambda e, pp=pp, bi=bi, jh=jh, tt=tt: e.matmul(
                                    pp[:], fq2[bi][:, 128 * jh:128 * (jh + 1)],
                                    fhsq[jh][:, TTILE * tt:TTILE * (tt + 1)],
                                    start=(jh == 0), stop=(jh == 7)),
                                   w=wts, i=[("PE", 1)] if jh == 7 else ())
                            pe_mm2 = val("PE")
                            st("vector", lambda e, pp=pp, dt=dt, tt=tt: e.scalar_tensor_tensor(
                                x[dt][:, TTILE * tt:TTILE * (tt + 1)], pp[:], sffn[:, 0:1],
                                x[dt][:, TTILE * tt:TTILE * (tt + 1)], ALU.mult, ALU.add),
                               w=[("PE", pe_mm2)], i=[("VE", 1)])
                return val("VE")

            if stage >= 2:
                ve_ln1 = emit_ln()
                emit_ffn(ve_ln1)

            if stage >= 3:
                ve_ln2 = emit_ln()

                rope_state = {"ve": {}, "dv": {}, "act": {}, "g": 0}

                def emit_proj(wsrc, ncout, bias_sb, do_rope, sink):
                    for co in range(ncout):
                        g = rope_state["g"]
                        bi = co % 2
                        qi = g % 2
                        st("sync", lambda e, co=co, wsrc=wsrc: e.dma_start(
                            out=kstg[:].rearrange("p (ci f) -> p ci f", ci=16),
                            in_=wsrc.rearrange("(ci p) o -> p ci o", p=128)[:, :, 128 * co:128 * (co + 1)]),
                           w=[("ACT", val("ACT"))], i=[("DS", 16)])
                        ds_w = val("DS")
                        st("scalar", lambda e, bi=bi: e.copy(kwt[bi][:], kstg[:]),
                           w=[("DS", ds_w), ("PE", val("PE"))], i=[("ACT", 1)])
                        act_w = val("ACT")
                        for tt in range(2):
                            pp = pb[tt]
                            for ci in range(NCH):
                                wts = [("ACT", act_w)] if ci == 0 else []
                                st("tensor", lambda e, pp=pp, bi=bi, ci=ci, tt=tt: e.matmul(
                                    pp[:], kwt[bi][:, 128 * ci:128 * (ci + 1)],
                                    lnb[ci][:, TTILE * tt:TTILE * (tt + 1)],
                                    start=(ci == 0), stop=(ci == NCH - 1)),
                                   w=wts, i=[("PE", 1)] if ci == NCH - 1 else ())
                        pe_mm = val("PE")
                        ev_w = [("PE", pe_mm), ("VE", rope_state["ve"].get(g - 2, 0)),
                                ("DV", rope_state["dv"].get(g - 2, 0))]
                        st("scalar", lambda e, co=co, qi=qi, bias_sb=bias_sb: e.activation(
                            kqs[qi][:, 0:TTILE], pb[0][:], AF.Identity,
                            bias=bias_sb[:, co:co + 1]), w=ev_w)
                        st("scalar", lambda e, co=co, qi=qi, bias_sb=bias_sb: e.activation(
                            kqs[qi][:, TTILE:TOK], pb[1][:], AF.Identity,
                            bias=bias_sb[:, co:co + 1]), i=[("ACT", 1)])
                        act_ev = val("ACT")
                        if debug and g == 0:
                            st("scalar", lambda e, qi=qi: e.dma_start(out=dbg_kqs, in_=kqs[qi][:]),
                               i=[("DV", 16)])
                            st("vector", lambda e: e.tensor_copy(dbgs[:], pb[0][:]),
                               w=[("PE", pe_mm)])
                            st("vector", lambda e, bi=bi: e.tensor_copy(dbgw[:], kwt[bi][:]))
                            st("scalar", lambda e: e.dma_start(out=dbg_ps, in_=dbgs[:]),
                               w=[("VE", val("VE"))], i=[("DV", 16)])
                            st("scalar", lambda e: e.dma_start(out=dbg_wt, in_=dbgw[:]),
                               i=[("DV", 16)])
                        ro_w = [("DV", rope_state["dv"].get(g - 2, 0)),
                                ("ACT", rope_state["act"].get(g - 2, 0))]
                        if do_rope:
                            for tt in range(2):
                                st("tensor", lambda e, qi=qi, tt=tt: e.matmul(
                                    pb[4 + tt][:], rotp[:], kqs[qi][:, TTILE * tt:TTILE * (tt + 1)],
                                    start=True, stop=True),
                                   w=[("ACT", act_ev), ("VE", val("VE")), ("DS", ds_x)],
                                   i=[("PE", 1)] if tt == 1 else ())
                            pe_rot = val("PE")
                            for tt in range(2):
                                o = TTILE * tt
                                st("vector", lambda e, qi=qi, o=o: e.tensor_mul(
                                    kt1[:], kqs[qi][:, o:o + TTILE], cost[:]),
                                   w=[("PE", pe_rot)] + (ro_w if tt == 0 else []))
                                st("vector", lambda e, tt=tt: e.tensor_mul(
                                    kt2[:], pb[4 + tt][:], sint[:]))
                                st("vector", lambda e, qi=qi, o=o: e.tensor_add(
                                    kro[qi][:, o:o + TTILE], kt1[:], kt2[:]), i=[("VE", 1)])
                        else:
                            st("vector", lambda e, qi=qi: e.tensor_copy(kro[qi][:], kqs[qi][:]),
                               w=[("ACT", act_ev)] + ro_w, i=[("VE", 1)])
                        rope_state["ve"][g] = val("VE")
                        sink(co, kro[qi])
                        rope_state["dv"][g] = val("DV")
                        rope_state["act"][g] = val("ACT")
                        rope_state["g"] = g + 1

                def q_sink(co, tile):
                    st("scalar", lambda e, co=co, tile=tile: e.dma_start(
                        out=qin[128 * co:128 * (co + 1), :], in_=tile[:]),
                       w=[("VE", val("VE"))], i=[("DV", 16)])
                emit_proj(wq, NCH, sbq, True, q_sink)

                def k_sink(co, tile):
                    st("scalar", lambda e, co=co, tile=tile: e.dma_start(
                        out=kin[128 * co:128 * (co + 1), :], in_=tile[:]),
                       w=[("VE", val("VE"))], i=[("DV", 16)])
                    st("scalar", lambda e, tile=tile: e.square(ksq[:], tile[:]),
                       w=[("VE", val("VE")), ("PE", val("PE"))], i=[("ACT", 1)])
                    act_sq = val("ACT")
                    for half in range(2):
                        kvh = 2 * co + half
                        po = 64 * half
                        for tt in range(2):
                            st("tensor", lambda e, po=po, tt=tt: e.matmul(
                                pb[6][0:1, :], onesc_b[po:po + 64, :],
                                ksq[po:po + 64, TTILE * tt:TTILE * (tt + 1)],
                                start=True, stop=True),
                               w=[("ACT", act_sq), ("VE", val("VE"))], i=[("PE", 1)])
                            st("vector", lambda e, tt=tt: e.tensor_copy(
                                knrow[0:1, TTILE * tt:TTILE * (tt + 1)], pb[6][0:1, :]),
                               w=[("PE", val("PE")), ("DV", val("DV"))], i=[("VE", 1)])
                        st("scalar", lambda e, kvh=kvh: e.dma_start(
                            out=kin[512 + kvh:513 + kvh, :], in_=knrow[:]),
                           w=[("VE", val("VE"))], i=[("DV", 16)])
                emit_proj(wk, 4, sbk, True, k_sink)

                def v_sink(co, tile):
                    st("vector", lambda e, co=co, tile=tile: e.tensor_copy(kvf[co][:], tile[:]),
                       i=[("VE", 1)])
                emit_proj(wv, 4, sbv, False, v_sink)
                dv_vt = {}
                for tg in range(8):
                    bi = tg % 2
                    for vc in range(4):
                        st("tensor", lambda e, tg=tg, vc=vc: e.transpose(
                            pbf[:, 128 * vc:128 * (vc + 1)], kvf[vc][:, 128 * tg:128 * (tg + 1)],
                            ident[:]),
                           w=[("VE", val("VE")), ("DS", ds_x)] if vc == 0 else [],
                           i=[("PE", 1)] if vc == 3 else ())
                    st("vector", lambda e, bi=bi: e.tensor_copy(kvt[bi][:], pbf[:, 0:512]),
                       w=[("PE", val("PE")), ("DV", dv_vt.get(tg - 2, 0))], i=[("VE", 1)])
                    st("scalar", lambda e, tg=tg, bi=bi: e.dma_start(
                        out=vin[128 * tg:128 * (tg + 1), :], in_=kvt[bi][:]),
                       w=[("VE", val("VE"))], i=[("DV", 16)])
                    dv_vt[tg] = val("DV")
                dv_qkv = val("DV")

                st("gpsimd", lambda e: e.collective_compute(
                    "AllGather", ALU.bypass, replica_groups=[list(range(NCORE))],
                    ins=[kin.opt()], outs=[kout.opt()]), w=[("DV", dv_qkv)], i=[("CC", 1)])
                st("gpsimd", lambda e: e.collective_compute(
                    "AllGather", ALU.bypass, replica_groups=[list(range(NCORE))],
                    ins=[vin.opt()], outs=[vout.opt()]), i=[("CC", 1)])
                st("gpsimd", lambda e: e.collective_compute(
                    "AllGather", ALU.bypass, replica_groups=[list(range(NCORE))],
                    ins=[qin.opt()], outs=[qout.opt()]), i=[("CC", 1)])
                cc_kvq = val("CC")

                for r in range(8):
                    st("gpsimd", lambda e, r=r: e.dma_start(
                        out=knsc[:], in_=kout[520 * r + 512:520 * r + 520, :]),
                       w=([("CC", cc_kvq)] if r == 0 else [("VE", val("VE"))]), i=[("DP", 16)])
                    if r == 0:
                        st("vector", lambda e: e.tensor_copy(knmx[:], knsc[:]),
                           w=[("DP", val("DP"))], i=[("VE", 1)])
                    else:
                        st("vector", lambda e: e.tensor_max(knmx[:], knmx[:], knsc[:]),
                           w=[("DP", val("DP"))], i=[("VE", 1)])
                st("vector", lambda e: e.reduce_max(knmax8[:], knmx[:], AX), i=[("VE", 1)])
                st("scalar", lambda e: e.dma_start(out=knb, in_=knmax8[:]),
                   w=[("VE", val("VE"))], i=[("DV", 16)])
                st("gpsimd", lambda e: e.dma_start(out=knrow8[:], in_=knb.rearrange("a b -> b a")),
                   w=[("DV", val("DV"))], i=[("DP", 16)])
                st("vector", lambda e: e.reduce_max(kn2max[:], knrow8[:], AX),
                   w=[("DP", val("DP"))], i=[("VE", 1)])
                ve_kn = val("VE")

                kview = kout.rearrange("a (s b) -> (a s) b", s=2)
                vview = vout.rearrange("a (s b) -> (a s) b", s=4)
                qview = qout.rearrange("a (s b) -> (a s) b", s=2)
                aview = aout.rearrange("a (s b) -> (a s) b", s=8)
                for kvl in range(2):
                    for r in range(8):
                        g = IDX_KDUP + kvl * 8 + r
                        st("gpsimd", lambda e, kvl=kvl, r=r, g=g: e.indirect_dma_start(
                            out=akd[kvl][:, 512 * r:512 * (r + 1)], out_offset=None,
                            in_=kview,
                            in_offset=bass.IndirectOffsetOnAxis(sidx[:, g:g + 1], 0)),
                           w=[("CC", cc_kvq), ("DS", ds_x)], i=[("DP", 16)])
                ve_vcp = {}
                for tc in range(32):
                    st("vector", lambda e, tc=tc: e.memset(
                        av65[tc][:, 0:130].rearrange("p (g f) -> p g f", g=2)[:, :, 64:65], 1.0),
                       i=[("VE", 1)])
                    g = IDX_V65 + tc
                    bi2 = tc % 2
                    st("gpsimd", lambda e, bi2=bi2, g=g: e.indirect_dma_start(
                        out=av2[bi2][:], out_offset=None, in_=vview,
                        in_offset=bass.IndirectOffsetOnAxis(sidx[:, g:g + 1], 0)),
                       w=[("CC", cc_kvq), ("VE", ve_vcp.get(tc - 2, 0))], i=[("DP", 16)])
                    st("vector", lambda e, tc=tc, bi2=bi2: e.tensor_copy(
                        av65[tc][:, 0:130].rearrange("p (g f) -> p g f", g=2)[:, :, 0:64],
                        av2[bi2][:].rearrange("p (g f) -> p g f", g=2)),
                       w=[("DP", val("DP"))])
                    ve_vcp[tc] = val("VE")
                dp_kv = val("DP")
                ve_v65 = val("VE")

                act_prev_qt = val("ACT")
                dv_fin_prev = val("DV")
                for jj in range(4):
                    qb = jj % 2
                    for r in range(8):
                        g = IDX_Q + jj * 8 + r
                        st("gpsimd", lambda e, qb=qb, r=r, g=g: e.indirect_dma_start(
                            out=aq[qb][:, 512 * r:512 * (r + 1)], out_offset=None,
                            in_=qview,
                            in_offset=bass.IndirectOffsetOnAxis(sidx[:, g:g + 1], 0)),
                           w=([("PE", val("PE")), ("ACT", val("ACT"))] if jj >= 2
                              else [("CC", cc_kvq)]),
                           i=[("DP", 16)])
                    dp_q = val("DP")
                    kvl = jj // 2
                    for qt in range(8):
                        st("scalar", lambda e, qb=qb, qt=qt: e.square(
                            aqsq[:], aq[qb][:, TTILE * qt:TTILE * (qt + 1)]),
                           w=[("DP", dp_q), ("VE", val("VE")), ("PE", val("PE"))], i=[("ACT", 1)])
                        act_qs = val("ACT")
                        for h in range(2):
                            po = 64 * h
                            st("tensor", lambda e, po=po: e.matmul(
                                pb[6][0:1, :], onesc_b[po:po + 64, :], aqsq[po:po + 64, :],
                                start=True, stop=True),
                               w=[("ACT", act_qs), ("VE", val("VE"))], i=[("PE", 1)])
                            st("vector", lambda e: e.reduce_max(srow[0:1, 9:10], pb[6][0:1, :], AX),
                               w=[("PE", val("PE")), ("VE", ve_kn), ("ACT", act_prev_qt)],
                               i=[("VE", 1)])
                            st("vector", lambda e: e.tensor_mul(
                                srow[0:1, 10:11], srow[0:1, 9:10], kn2max[:]),
                               w=[("VE", val("VE"))], i=[("VE", 1)])
                            st("scalar", lambda e: e.activation(
                                srow[0:1, 11:12], srow[0:1, 10:11], AF.Sqrt),
                               w=[("VE", val("VE"))], i=[("ACT", 1)])
                            st("vector", lambda e: e.tensor_scalar_mul(
                                srow[0:1, 12:13], srow[0:1, 11:12], -0.125),
                               w=[("ACT", val("ACT"))], i=[("VE", 1)])
                            st("tensor", lambda e: e.matmul(
                                pb[6][:, 0:1], onesr_f[:], srow[0:1, 12:13],
                                start=True, stop=True),
                               w=[("VE", val("VE"))], i=[("PE", 1)])
                            st("vector", lambda e, h=h: e.tensor_copy(cb[:, h:h + 1], pb[6][:, 0:1]),
                               w=[("PE", val("PE"))], i=[("VE", 1)])
                        ve_cb = val("VE")
                        nk = 4 * qt + 4
                        pe_pv_done = {}
                        act_exp = {}
                        for kc in range(nk):
                            rot = kc % 2
                            psA, psB = pb[0 + rot], pb[2 + rot]
                            ptA, ptB = apt[0 + rot], apt[2 + rot]
                            wts = [("VE", ve_cb), ("DP", dp_kv)]
                            if kc >= 2:
                                wts.append(("ACT", act_exp[kc - 2]))
                            st("tensor", lambda e, psA=psA, kvl=kvl, kc=kc, qb=qb, qt=qt: e.matmul(
                                psA[:], akd[kvl][0:64, 128 * kc:128 * (kc + 1)],
                                aq[qb][0:64, TTILE * qt:TTILE * (qt + 1)],
                                start=True, stop=True, tile_position=(0, 0)), w=wts)
                            st("tensor", lambda e, psB=psB, kvl=kvl, kc=kc, qb=qb, qt=qt: e.matmul(
                                psB[:], akd[kvl][64:128, 128 * kc:128 * (kc + 1)],
                                aq[qb][64:128, TTILE * qt:TTILE * (qt + 1)],
                                start=True, stop=True, tile_position=(64, 0)), i=[("PE", 1)])
                            pe_s = val("PE")
                            e_w = [("PE", pe_s)]
                            if kc >= 2:
                                e_w.append(("PE", pe_pv_done[kc - 2]))
                            st("scalar", lambda e, psA=psA, ptA=ptA: e.activation(
                                ptA[:], psA[:], AF.Exp, bias=cb[:, 0:1], scale=0.125), w=e_w)
                            st("scalar", lambda e, psB=psB, ptB=ptB: e.activation(
                                ptB[:], psB[:], AF.Exp, bias=cb[:, 1:2], scale=0.125),
                               i=[("ACT", 2)])
                            act_exp[kc] = val("ACT")
                            pv_w = [("ACT", act_exp[kc]), ("DP", dp_kv), ("VE", ve_v65)]
                            if kc >= 4 * qt:
                                mj = kc - 4 * qt
                                st("vector", lambda e, ptA=ptA, mj=mj: e.tensor_mul(
                                    ptA[:], ptA[:], masks[:, TTILE * mj:TTILE * (mj + 1)]),
                                   w=[("ACT", act_exp[kc])])
                                st("vector", lambda e, ptB=ptB, mj=mj: e.tensor_mul(
                                    ptB[:], ptB[:], masks[:, TTILE * mj:TTILE * (mj + 1)]),
                                   i=[("VE", 1)])
                                pv_w.append(("VE", val("VE")))
                            for h, pt in ((0, ptA), (1, ptB)):
                                st("tensor", lambda e, h=h, pt=pt, kc=kc, kvl=kvl, nk=nk: e.matmul(
                                    pb[4 + h][0:65, :], av65[kc][:, 65 * kvl:65 * kvl + 65], pt[:],
                                    start=(kc == 0), stop=(kc == nk - 1)),
                                   w=(pv_w if h == 0 else ()), i=[("PE", 1)] if h == 1 else ())
                            pe_pv_done[kc] = val("PE")
                        pe_pv = val("PE")
                        for h in range(2):
                            rbt = arb[h]
                            aot = aao[h]
                            st("vector", lambda e, h=h: e.reciprocal(
                                rrow_at[0:1, :], pb[4 + h][64:65, :]),
                               w=[("PE", pe_pv), ("DV", dv_fin_prev)], i=[("VE", 1)])
                            st("tensor", lambda e: e.matmul(
                                pb[6][0:64, :], onesr_f[0:1, 0:64], rrow_at[0:1, :],
                                start=True, stop=True),
                               w=[("VE", val("VE"))], i=[("PE", 1)])
                            st("vector", lambda e, rbt=rbt: e.tensor_copy(rbt[:], pb[6][0:64, :]),
                               w=[("PE", val("PE"))])
                            st("vector", lambda e, h=h, rbt=rbt, aot=aot: e.tensor_mul(
                                aot[:], pb[4 + h][0:64, :], rbt[:]), i=[("VE", 1)])
                            row = 64 * (2 * jj + h)
                            st("scalar", lambda e, row=row, qt=qt, aot=aot: e.dma_start(
                                out=ain[row:row + 64, TTILE * qt:TTILE * (qt + 1)], in_=aot[:]),
                               w=[("VE", val("VE"))], i=[("DV", 16)])
                        dv_fin_prev = val("DV")
                        act_prev_qt = val("ACT")
                        st("tensor", fn=None, w=[("VE", val("VE"))])
                if debug:
                    st("scalar", lambda e: e.dma_start(out=dbg_akd, in_=akd[0][:]),
                       w=[("PE", val("PE")), ("VE", val("VE"))], i=[("DV", 16)])
                    st("scalar", lambda e: e.dma_start(out=dbg_aq, in_=aq[1][:]), i=[("DV", 16)])
                    st("scalar", lambda e: e.dma_start(
                        out=dbg_v65.rearrange("p (tc f) -> p tc f", tc=4),
                        in_=arena[0:128, 8192:8712].rearrange("p (tc f) -> p tc f", tc=4)),
                       i=[("DV", 16)])
                dv_ain = val("DV")
                st("gpsimd", lambda e: e.collective_compute(
                    "AllGather", ALU.bypass, replica_groups=[list(range(NCORE))],
                    ins=[ain.opt()], outs=[aout.opt()]), w=[("DV", dv_ain)], i=[("CC", 1)])
                cc_ao = val("CC")

                for c in range(NCH):
                    for b2 in range(2):
                        g = IDX_AO + c * 2 + b2
                        st("gpsimd", lambda e, c=c, b2=b2, g=g: e.indirect_dma_start(
                            out=wao[c][:, TTILE * b2:TTILE * (b2 + 1)], out_offset=None,
                            in_=aview,
                            in_offset=bass.IndirectOffsetOnAxis(sidx[:, g:g + 1], 0)),
                           w=[("CC", cc_ao)], i=[("DP", 16)])
                dp_ao = val("DP")
                for co in range(NCH):
                    bi = co % 2
                    st("sync", lambda e, co=co: e.dma_start(
                        out=wstg[:].rearrange("p (ci f) -> p ci f", ci=16),
                        in_=wo.rearrange("(ci p) o -> p ci o", p=128)[:, :, 128 * co:128 * (co + 1)]),
                       w=[("ACT", val("ACT")), ("DV", dv_ain), ("VE", val("VE"))], i=[("DS", 16)])
                    ds_w = val("DS")
                    st("scalar", lambda e, bi=bi: e.copy(wwt[bi][:], wstg[:]),
                       w=[("DS", ds_w), ("PE", val("PE"))], i=[("ACT", 1)])
                    act_w = val("ACT")
                    for tt in range(2):
                        pp = pb[(2 * co + tt) % 4]
                        for ci in range(NCH):
                            wts = [("ACT", act_w), ("DP", dp_ao), ("VE", val("VE"))] if ci == 0 else []
                            st("tensor", lambda e, pp=pp, bi=bi, ci=ci, tt=tt: e.matmul(
                                pp[:], wwt[bi][:, 128 * ci:128 * (ci + 1)],
                                wao[ci][:, TTILE * tt:TTILE * (tt + 1)],
                                start=(ci == 0), stop=(ci == NCH - 1)),
                               w=wts, i=[("PE", 1)] if ci == NCH - 1 else ())
                        pe_mm = val("PE")
                        st("vector", lambda e, pp=pp, co=co, tt=tt: e.scalar_tensor_tensor(
                            x[co][:, TTILE * tt:TTILE * (tt + 1)], pp[:], sbo[:, co:co + 1],
                            x[co][:, TTILE * tt:TTILE * (tt + 1)], ALU.add, ALU.add),
                           w=[("PE", pe_mm)], i=[("VE", 1)])

            if stage >= 4:
                ve_ln3 = emit_ln()
                emit_ffn(ve_ln3)

            ve_fin = val("VE")
            pe_fin = val("PE")
            act_fin = val("ACT")
            if debug:
                st("sync", lambda e: e.dma_start(out=dbg_srow, in_=srow[:]),
                   w=[("VE", ve_fin), ("PE", pe_fin), ("ACT", act_fin)], i=[("DS", 16)])
                st("sync", lambda e: e.dma_start(out=dbg_thr, in_=thr_bc[:]), i=[("DS", 16)])
                st("sync", lambda e: e.dma_start(out=dbg_ln, in_=lnb[0][:]), i=[("DS", 16)])
                st("sync", lambda e: e.dma_start(out=dbg_q1g, in_=q1g), i=[("DS", 16)])
                if stage >= 3:
                    st("sync", lambda e: e.dma_start(out=dbg_kn, in_=kn2max[:]), i=[("DS", 16)])
                    st("sync", lambda e: e.dma_start(out=dbg_cb, in_=cb[:]), i=[("DS", 16)])
                    st("sync", lambda e: e.dma_start(out=dbg_qin, in_=qin), i=[("DS", 16)])
                    st("sync", lambda e: e.dma_start(out=dbg_kin, in_=kin), i=[("DS", 16)])
                    st("sync", lambda e: e.dma_start(out=dbg_ain, in_=ain), i=[("DS", 16)])
            for c in range(NCH):
                st("sync", lambda e, c=c: e.dma_start(out=xo[128 * c:128 * (c + 1), :], in_=x[c][:]),
                   w=[("VE", ve_fin), ("PE", pe_fin), ("ACT", act_fin)], i=[("DS", 16)])
            st("sync", fn=None, w=[("DS", val("DS")), ("DV", val("DV")), ("DP", val("DP"))])

        for s in cnt:
            sems[s] = es.enter_context(nc.semaphore(s))
        with nc.Block() as block:
            build_schedule()
            engmap = {"sync": block.sync, "tensor": block.tensor, "scalar": block.scalar,
                      "vector": block.vector, "gpsimd": block.gpsimd}
            for name in engmap:
                def body(eng, name=name):
                    lastw = {}
                    veseen = 0
                    for (e, fn, waits, incs) in steps:
                        if e != name:
                            continue
                        if name == "vector" and fn is not None and veseen > 0 \
                                and lastw.get("VE", -1) < veseen:
                            eng.wait_ge(sems["VE"], veseen)
                            lastw["VE"] = veseen
                        for (s, v) in waits:
                            if v > 0 and lastw.get(s, -1) < v:
                                eng.wait_ge(sems[s], v)
                                lastw[s] = v
                        ins = fn(eng) if fn is not None else None
                        for (s, a) in incs:
                            if ins is None:
                                ins = eng.engine_nop()
                            ins = ins.then_inc(sems[s], a)
                            if name == "vector" and s == "VE":
                                veseen += a
                engmap[name](body)
    return nc


def host_prep(inputs):
    x = np.asarray(inputs["L0_attn_x"], np.float32)
    Wq = np.ascontiguousarray(np.asarray(inputs["L0_attn_Wq"], np.float32))
    bq = np.asarray(inputs["L0_attn_bq"], np.float32)
    Wk = np.ascontiguousarray(np.asarray(inputs["L0_attn_Wk"], np.float32))
    bk = np.asarray(inputs["L0_attn_bk"], np.float32)
    Wv = np.ascontiguousarray(np.asarray(inputs["L0_attn_Wv"], np.float32))
    bv = np.asarray(inputs["L0_attn_bv"], np.float32)
    Wo = np.ascontiguousarray(np.asarray(inputs["L0_attn_Wo"], np.float32))
    bo = np.asarray(inputs["L0_attn_bo"], np.float32)
    wfc = np.asarray(inputs["L0_ffn_w_fc"], np.float32)
    wpr = np.asarray(inputs["L0_ffn_w_proj"], np.float32)

    wfcT = np.ascontiguousarray(wfc.T)
    wprT = np.ascontiguousarray(wpr.T)

    bq_r = np.ascontiguousarray(bq.reshape(NCH, 128).T)
    bk_r = np.ascontiguousarray(bk.reshape(4, 128).T)
    bv_r = np.ascontiguousarray(bv.reshape(4, 128).T)
    bo_r = np.ascontiguousarray(bo.reshape(NCH, 128).T)

    inv = 1.0 / (10000.0 ** (np.arange(0, HD, 2, dtype=np.float64) / HD))
    freq_per_dim = np.repeat(inv, 2)
    q_idx = np.arange(TTILE)[None, :]
    k_idx = np.arange(128)[:, None]
    masksl = [(q_idx >= k_idx + 128 * j).astype(np.float32) for j in range(4)]
    masks = np.concatenate(masksl, 1).astype(ml_dtypes.bfloat16)
    rot = np.zeros((128, 128), np.float32)
    for i2 in range(64):
        rot[2 * i2 + 1, 2 * i2] = -1.0
        rot[2 * i2, 2 * i2 + 1] = 1.0
    rot = rot.astype(ml_dtypes.bfloat16)
    ident = np.eye(128, dtype=np.float32).astype(ml_dtypes.bfloat16)

    p = np.arange(128)
    maps = []
    for i in range(NCORE):
        u = i % 4
        BA = i // 4
        xs = np.concatenate([x[0, 512 * i:512 * (i + 1)], x[1, 512 * i:512 * (i + 1)]], 0)
        x_tc = np.ascontiguousarray(xs.T)
        pos = np.arange(512 * i, 512 * (i + 1), dtype=np.float64)
        ang = pos[None, :] * freq_per_dim[p % 64, None]
        costab = np.cos(ang).astype(np.float32)
        sintab = np.sin(ang).astype(np.float32)
        idx = np.zeros((128, 112), np.int64)
        for kvl in range(2):
            for r in range(8):
                idx[:, IDX_KDUP + kvl * 8 + r] = 2 * (520 * r + 128 * u + 64 * kvl + (p % 64)) + BA
        for tc in range(32):
            r = tc // 4
            idx[:, IDX_V65 + tc] = 4 * (1024 * r + 512 * BA + 128 * (tc % 4) + p) + u
        for jj in range(4):
            for r in range(8):
                idx[:, IDX_Q + jj * 8 + r] = 2 * (2048 * r + 512 * u + 128 * jj + p) + BA
        for c in range(NCH):
            for b2 in range(2):
                d = 128 * c + p
                h = d // 64
                j = 4 * b2 + h // 8
                row = 512 * j + 64 * (h % 8) + (d % 64)
                idx[:, IDX_AO + c * 2 + b2] = 8 * row + i
        maps.append({
            "x_t": x_tc, "wq": Wq, "wk": Wk, "wv": Wv, "wo": Wo,
            "wfc_sh": np.ascontiguousarray(wfcT[256 * i:256 * (i + 1)]),
            "wpr_sh": np.ascontiguousarray(wprT[640 * i:640 * (i + 1)]),
            "bq_r": bq_r, "bk_r": bk_r, "bv_r": bv_r, "bo_r": bo_r,
            "costab": costab, "sintab": sintab, "maskin": masks,
            "rotin": rot, "idin": ident, "idxin": idx.astype(np.int32),
        })
    return maps


def assemble(results):
    out = np.zeros((B, T, D), np.float32)
    for i in range(NCORE):
        xt = results[i]["xo"]
        out[0, 512 * i:512 * (i + 1)] = xt[:, 0:512].T
        out[1, 512 * i:512 * (i + 1)] = xt[:, 512:1024].T
    return out


def kernel(**inputs):
    maps = host_prep(inputs)
    nc = build_nc(stage=4)
    res = run_bass_kernel_spmd(nc, maps, core_ids=list(range(NCORE)))
    return assemble(res.results)
